# version 29
# baseline (speedup 1.0000x reference)
import numpy as np
import ml_dtypes
from contextlib import ExitStack

import concourse.bass as bass
import concourse.mybir as mybir
from concourse.bass_utils import run_bass_kernel_spmd

F32 = mybir.dt.float32
BF16 = mybir.dt.bfloat16
AX = mybir.AxisListType.X
ALU = mybir.AluOpType
AF = mybir.ActivationFunctionType

MAGIC = 12582912.0
D = 1024
NCH = 8
NT = 16
TOKS = NT * 128
SEQ = 4096
EPS = 1e-5
INVN = 1.0 / D

_CACHE = {}


def _build():
    nc = bass.Bass("TRN2", target_bir_lowering=False)

    x_h = nc.dram_tensor("x", [TOKS, D], F32, kind="ExternalInput")
    wf_h = nc.dram_tensor("wf", [D, D], F32, kind="ExternalInput")
    wv_h = nc.dram_tensor("wv", [D, D], F32, kind="ExternalInput")
    wo_h = nc.dram_tensor("wo", [D, D], F32, kind="ExternalInput")
    bv_h = nc.dram_tensor("bv", [1, D], F32, kind="ExternalInput")
    bo_h = nc.dram_tensor("bo", [1, D], F32, kind="ExternalInput")
    ccin = nc.dram_tensor("ccin", [128, NCH], F32, kind="Internal")
    ccout = nc.dram_tensor("ccout", [2 * 128, NCH], F32, kind="Internal")
    idb_h = nc.dram_tensor("idb", [128, 128], BF16, kind="ExternalInput")
    idf_h = nc.dram_tensor("idf", [4, 4], F32, kind="ExternalInput")
    cst_h = nc.dram_tensor("cst", [128, 2], F32, kind="ExternalInput")
    out_h = nc.dram_tensor("out", [TOKS, D], F32, kind="ExternalOutput")

    with ExitStack() as st:
        sb = lambda name, shape, dt: st.enter_context(nc.sbuf_tensor(name, shape, dt))
        ps = lambda name, shape, dt: st.enter_context(nc.psum_tensor(name, shape, dt))

        x = sb("xres", [128, NT * D], F32)
        x1b = sb("x1b", [128, NT * D], BF16)
        wft = sb("wft", [128, NCH * D], BF16)
        wfr = sb("wfr", [128, NCH * D], F32)
        wfrb = wfr.bitcast(BF16)
        qbf = sb("qbf", [128, 2 * D], F32)
        wtc = sb("wtc", [128, 2 * D], BF16)
        xn = sb("xn", [128, 4 * D], BF16)
        xnt = sb("xnt", [128, 2 * D], BF16)
        sqs = sb("sqs", [128, D], F32)
        bcast = sb("bcast", [128, D], F32)
        ccsb = sb("ccsb", [128, 2 * NCH], F32)
        meanT = sb("meanT", [128, NCH], F32)
        meanTb = sb("meanTb", [128, NCH], BF16)
        csum = sb("csum", [128, NCH], F32)
        vbar = sb("vbar", [1, D], F32)
        vbarT = sb("vbarT", [128, NCH], BF16)
        bv4 = sb("bv4", [1, D], F32)
        bo4 = sb("bo4", [1, D], F32)
        idb = sb("idb_s", [128, 128], BF16)
        idf = sb("idf_s", [4, 4], F32)
        cst = sb("cst_s", [128, 2], F32)
        onescol = sb("onescol", [128, 1], F32)
        q4col = sb("q4col", [128, 1], F32)
        onesrow = sb("onesrow", [1, 128], F32)
        scp = sb("scp", [1, 4], F32)
        bcp = sb("bcp", [128, 4], F32)
        absac = sb("absac", [128, NCH], F32)
        abscol = sb("abscol", [128, 1], F32)
        rmred = sb("rmred", [128, 1], F32)
        bsum = sb("bsum", [128, 1], F32)
        stt = {}
        for stg in (0, 1, 2):
            stt[("msss", stg)] = sb(f"st_msss{stg}", [128, 2 * NT], F32)
            stt[("mE", stg)] = sb(f"st_mE{stg}", [128, 2 * NT], F32)
            stt[("p", stg)] = sb(f"st_p{stg}", [128, NT], F32)
            stt[("vp", stg)] = sb(f"st_vp{stg}", [128, NT], F32)
            stt[("rs", stg)] = sb(f"st_rs{stg}", [128, NT], F32)
        for stg in (0, 2):
            stt[("nb", stg)] = sb(f"st_nb{stg}", [128, NT], F32)
        rm = sb("st_rm", [128, NT], F32)
        rs1b = sb("rs1b", [128, NT], BF16)

        psF = [ps("psF0", [128, D], F32), ps("psF1", [128, D], F32)]
        psT = [ps("psT0", [128, D], BF16), ps("psT1", [128, D], BF16)]
        psCS = ps("psCS", [128, NCH], F32)
        psSm = ps("psSm", [128, 2], F32)

        nc.const_aps.register(F32, EPS, cst[:, 0:1])
        nc.const_aps.register(F32, MAGIC, cst[:, 1:2])

        sems = {}
        semnames = (["v", "s", "pe", "pl", "scc", "qsm", "qwv", "qwo",
                     "qcc", "qcr", "qy"]
                    + [f"qx{g}" for g in range(4)]
                    + [f"qw{q}" for q in range(4)])
        for nm in semnames:
            sems[nm] = st.enter_context(nc.semaphore(nm))

        block = st.enter_context(nc.Block())

        sched = []
        cnt = {k: 0 for k in sems}
        land = {}
        sqs_chain = [None]

        def emit_ln0_reduces(t):
            xt = x[:, t * D:(t + 1) * D]
            if f"a1_{t}" not in land:
                op("vector", (lambda xt=xt, t=t: lambda e: e.tensor_reduce(
                    stt[("msss", 0)][:, 2 * t:2 * t + 1], xt, AX, ALU.add))(),
                   waits=[f"ld_xg{t // 4}"], key=f"a1_{t}")
            op("scalar", (lambda xt=xt, t=t: lambda e: e.activation(
                sqs[:, :], xt, AF.Square,
                accum_out=stt[("msss", 0)][:, 2 * t + 1:2 * t + 2]))(),
               waits=[f"ld_xg{t // 4}", sqs_chain[0]], key=f"b1_{t}")
            sqs_chain[0] = f"b1_{t}"
        ENG_SEM = {"vector": "v", "scalar": "s", "tensor": "pe", "gpsimd": "pl",
                   "sync": "qy"}

        def op(eng, fn, waits=(), key=None, dma=None, noinc=False):
            ws = [w for w in waits if w is not None]
            if noinc:
                sched.append((eng, ws, fn, None, 0))
                return None
            sem = dma if dma else ENG_SEM[eng]
            amt = 16 if dma else 1
            sched.append((eng, ws, fn, sem, amt))
            cnt[sem] += amt
            if key:
                land[key] = (sem, cnt[sem])
            return (sem, cnt[sem])

        for pair in range(4):
            g2 = 2 * pair
            op("sync" if pair < 2 else "scalar",
               (lambda g2=g2: lambda e: e.dma_start(
                   out=wfr.ap()[:, g2 * D:(g2 + 2) * D]
                        .rearrange("p (c d) -> p c d", d=D),
                   in_=wf_h.ap()[g2 * 128:(g2 + 2) * 128, :]
                        .rearrange("(c p) d -> p c d", p=128)))(),
               dma=f"qw{pair}", key=f"wfq{pair}")
        op("scalar", lambda e: e.dma_start(out=idb[:, :], in_=idb_h[:, :]),
           dma="qsm")
        op("scalar", lambda e: e.dma_start(out=cst[:, :], in_=cst_h[:, :]),
           dma="qsm")
        op("scalar", lambda e: e.dma_start(out=idf[:, :], in_=idf_h[:, :]),
           dma="qsm")
        op("scalar", lambda e: e.dma_start(out=bv4[0:1, :], in_=bv_h[0:1, :]),
           dma="qsm")
        op("scalar", lambda e: e.dma_start(out=bo4[0:1, :], in_=bo_h[0:1, :]),
           dma="qsm", key="ld_sm")
        def emit_xg(g, eng):
            op(eng,
               (lambda gg: lambda e: e.dma_start(
                   out=x.ap()[:, gg * 4 * D:(gg + 1) * 4 * D]
                        .rearrange("p (t d) -> p t d", d=D),
                   in_=x_h.ap()[gg * 512:(gg + 1) * 512, :]
                        .rearrange("(t p) d -> p t d", p=128)))(g),
               dma=f"qx{g}", key=f"ld_xg{g}")
        emit_xg(0, "sync")
        emit_xg(1, "sync")
        emit_xg(2, "scalar")
        emit_xg(3, "sync")

        op("gpsimd", lambda e: e.dma_start(
            out=bass.AP(wfrb, 0, [[2 * NCH * D, 128], [D, NCH], [1, D]]),
            in_=wv_h.ap().rearrange("(c p) o -> p c o", p=128)),
           waits=["sq1_7"], dma="qwv", key="ld_wv")
        op("gpsimd", lambda e: e.dma_start(
            out=bass.AP(wfrb, NCH * D, [[2 * NCH * D, 128], [D, NCH], [1, D]]),
            in_=wo_h.ap().rearrange("(c p) o -> p c o", p=128)),
           waits=[("qwv", 16)], dma="qwo", key="ld_wo")

        op("vector", lambda e: e.memset(onescol[:, :], 1.0), key="ones")
        op("vector", lambda e: e.memset(q4col[:, :], 1.0 / SEQ), key="q4")
        op("vector", lambda e: e.memset(onesrow[:, :], 1.0), key="ones2")

        def emit_a1(t):
            op("vector", (lambda t=t: lambda e: e.tensor_reduce(
                stt[("msss", 0)][:, 2 * t:2 * t + 1],
                x[:, t * D:(t + 1) * D], AX, ALU.add))(),
               waits=[f"ld_xg{t // 4}"], key=f"a1_{t}")

        def emit_b1(t):
            op("scalar", (lambda t=t: lambda e: e.activation(
                sqs[:, :], x[:, t * D:(t + 1) * D], AF.Square,
                accum_out=stt[("msss", 0)][:, 2 * t + 1:2 * t + 2]))(),
               waits=[f"ld_xg{t // 4}", sqs_chain[0]], key=f"b1_{t}")
            sqs_chain[0] = f"b1_{t}"

        for c in (0, 4, 1, 5, 2, 6, 3, 7):
            op("scalar", (lambda cc: lambda e: e.activation(
                sqs[:, :], wfr[:, cc * D:(cc + 1) * D], AF.Abs,
                accum_out=absac[:, cc:cc + 1]))(c),
               waits=[f"wfq{c // 2}", sqs_chain[0]],
               key=f"abs{c}")
            sqs_chain[0] = f"abs{c}"
        op("vector", lambda e: e.tensor_reduce(
            abscol[:, :], absac[:, :], AX, ALU.add),
           waits=["abs7"], key="abscol")
        op("tensor", lambda e: e.matmul(
            psSm[0:1, 0:1], abscol[:, :], onescol[:, :], start=True, stop=True),
           waits=["abscol", "ones"], key="pe_sc")
        op("vector", lambda e: e.tensor_scalar(
            scp[0:1, 0:1], psSm[0:1, 0:1], 1.0 / (D * D), EPS, ALU.mult, ALU.max),
           waits=["pe_sc"], key="sc0")
        op("vector", lambda e: e.reciprocal(scp[0:1, 1:2], scp[0:1, 0:1]),
           waits=["sc0"], key="invk")
        op("vector", lambda e: e.tensor_scalar(
            scp[0:1, 2:3], scp[0:1, 0:1], 0.5, None, ALU.mult),
           waits=["sc0"], key="hs")
        op("tensor", lambda e: e.matmul(
            psSm[:, 0:2], onesrow[0:1, :], scp[0:1, 1:3], start=True, stop=True),
           waits=["hs", "invk", "ones2"], key="pe_bc")
        op("vector", lambda e: e.tensor_copy(bcp[:, 0:2], psSm[:, :]),
           waits=["pe_bc"], key="bcp")
        invb = bcp[:, 0:1]
        alphab = bcp[:, 1:2]
        corrb = bcp[:, 2:3]
        bs1024 = bcp[:, 3:4]

        def even(tns, g, n=4):
            return bass.AP(tns, 8 * g, [[tns.shape[1], 128], [2, n]])

        def odd(tns, g, n=4):
            return bass.AP(tns, 8 * g + 1, [[tns.shape[1], 128], [2, n]])

        msss0, mE0 = stt[("msss", 0)], stt[("mE", 0)]

        def emit_stats0q(q):
            sl = slice(4 * q, 4 * q + 4)
            op("vector", (lambda q=q: lambda e: e.tensor_scalar(
                mE0[:, 8 * q:8 * q + 8], msss0[:, 8 * q:8 * q + 8],
                INVN, None, ALU.mult))(),
               waits=[f"a1_{4 * q + 3}", f"b1_{4 * q + 3}"], key=f"amA{q}")
            op("vector", (lambda q=q, sl=sl: lambda e: e.scalar_tensor_tensor(
                stt[("p", 0)][:, sl], even(mE0, q), 1.0, even(mE0, q),
                ALU.bypass, ALU.mult))(),
               waits=[f"amA{q}"], key=f"apA{q}")
            op("vector", (lambda q=q, sl=sl: lambda e: e.tensor_tensor(
                stt[("vp", 0)][:, sl], odd(mE0, q), stt[("p", 0)][:, sl],
                ALU.subtract))(),
               waits=[f"apA{q}"], key=f"avpA{q}")

        def emit_stats0q_s(q):
            sl = slice(4 * q, 4 * q + 4)
            op("scalar", (lambda sl=sl: lambda e: e.activation(
                stt[("vp", 0)][:, sl], stt[("vp", 0)][:, sl], AF.Ln,
                bias=EPS))(),
               waits=[f"avpA{q}", "ld_sm"], key=f"alnA{q}")
            op("scalar", (lambda sl=sl: lambda e: e.activation(
                stt[("rs", 0)][:, sl], stt[("vp", 0)][:, sl], AF.Exp,
                scale=-0.5))(),
               waits=[f"alnA{q}"], key=f"aexpA{q}")

        def emit_stats0q_nb(q):
            sl = slice(4 * q, 4 * q + 4)
            op("vector", (lambda q=q, sl=sl: lambda e: e.scalar_tensor_tensor(
                stt[("nb", 0)][:, sl], even(mE0, q), -1.0,
                stt[("rs", 0)][:, sl], ALU.mult, ALU.mult))(),
               waits=[f"aexpA{q}", f"amA{q}"], key=f"anbA{q}")

        def emit_apply0(t):
            b4_ = t % 4
            op("vector", (lambda t=t, b4_=b4_: lambda e: e.tensor_scalar(
                xn[:, b4_ * D:(b4_ + 1) * D], x[:, t * D:(t + 1) * D],
                stt[("rs", 0)][:, t:t + 1], stt[("nb", 0)][:, t:t + 1],
                ALU.mult, ALU.add))(),
               waits=["anbA0" if t < 4 else ("anbA1" if t < 8 else "anbB"),
                      f"c1_{t - 4}" if t >= 4 else None],
               key=f"b4_{t}")

        def emit_twe(c):
            op("vector", (lambda cc=c: lambda e: e.tensor_copy(
                bass.AP(wft, cc * 128,
                        [[wft.shape[1], 128], [1024, NCH], [1, 128]]),
                bass.AP(psT[0], 0,
                        [[psT[0].shape[1], 128], [128, NCH], [1, 128]])))(),
               waits=[f"tw{c}"], key=f"twe{c}")

        for c in range(NCH):
            b2 = c % 2
            op("scalar", (lambda cc, b2=b2: lambda e: e.activation(
                qbf[:, b2 * D:(b2 + 1) * D], wfr[:, cc * D:(cc + 1) * D],
                AF.Identity, bias=MAGIC, scale=invb))(c),
               waits=["bcp", "ld_sm",
                      f"q3_{c - 2}" if c >= 2 else None],
               key=f"sq1_{c}")
            op("vector", (lambda b2=b2: lambda e: e.tensor_scalar(
                qbf[:, b2 * D:(b2 + 1) * D], qbf[:, b2 * D:(b2 + 1) * D],
                -MAGIC, -1.0, ALU.add, ALU.max))(),
               waits=[f"sq1_{c}"], key=f"q2_{c}")
            op("vector", (lambda b2=b2: lambda e: e.tensor_scalar(
                wtc[:, b2 * D:(b2 + 1) * D], qbf[:, b2 * D:(b2 + 1) * D],
                1.0, None, ALU.min))(),
               waits=[f"q2_{c}", f"tw{c - 2}" if c >= 2 else None],
               key=f"q3_{c}")
            if c >= 1:
                emit_twe(c - 1)
            if c == 1:
                emit_a1(0)
                emit_a1(1)
            elif c == 2:
                emit_a1(2)
                emit_a1(3)
            elif c == 3:
                emit_stats0q(0)
            elif c == 4:
                emit_stats0q_nb(0)
                emit_apply0(0)
            elif c == 5:
                emit_apply0(1)
                emit_apply0(2)
            elif c == 6:
                emit_a1(4)
            elif c == 7:
                emit_a1(5)

            def tw(cc, bb):
                def f(e):
                    ins = None
                    for k in range(NCH):
                        ins = e.transpose(
                            psT[0][:, k * 128:(k + 1) * 128],
                            wtc[:, bb * D + k * 128: bb * D + (k + 1) * 128],
                            idb[:, :])
                    return ins
                return f
            op("tensor", tw(c, b2),
               waits=[f"q3_{c}", "ld_sm", f"twe{c - 1}" if c >= 1 else None],
               key=f"tw{c}")

            if c == 1:
                emit_b1(0)
            elif c == 2:
                emit_b1(1)
                emit_b1(2)
            elif c == 3:
                emit_b1(3)
            elif c == 4:
                emit_b1(4)
                emit_stats0q_s(0)
            elif c == 5:
                emit_b1(5)
            elif c == 6:
                emit_b1(6)
        emit_twe(NCH - 1)
        WFT_READY = f"twe{NCH - 1}"
        emit_a1(6)
        emit_a1(7)
        emit_b1(7)
        emit_stats0q(1)

        def transposes(t, phase):
            b2 = t % 2
            b4_ = t % 4

            def f(e):
                ins = None
                for c in range(NCH):
                    ins = e.transpose(
                        psT[b2][:, c * 128:(c + 1) * 128],
                        xn[:, b4_ * D + c * 128: b4_ * D + (c + 1) * 128],
                        idb[:, :])
                return ins
            return f

        def ffn_mm(t):
            b = t % 2

            def f(e):
                ins = None
                for c in range(NCH):
                    for h in range(2):
                        ins = e.matmul(
                            psF[b][:, h * 512:(h + 1) * 512],
                            xnt[:, b * D + c * 128: b * D + (c + 1) * 128],
                            wft[:, c * D + h * 512: c * D + h * 512 + 512],
                            start=(c == 0), stop=(c == NCH - 1))
                return ins
            return f

        def emit_stats0(half):
            h8 = 8 * half
            sfx = "A" if half == 0 else "B"
            sl = slice(h8, h8 + 8)
            op("vector", lambda e: e.tensor_scalar(
                mE0[:, 2 * h8:2 * h8 + 16], msss0[:, 2 * h8:2 * h8 + 16],
                INVN, None, ALU.mult),
               waits=[f"a1_{h8 + 7}", f"b1_{h8 + 7}"], key=f"am{sfx}")
            op("vector", lambda e: e.scalar_tensor_tensor(
                stt[("p", 0)][:, sl], even(mE0, half * 2, 8), 1.0,
                even(mE0, half * 2, 8), ALU.bypass, ALU.mult),
               waits=[f"am{sfx}"], key=f"ap{sfx}")
            op("vector", lambda e: e.tensor_tensor(
                stt[("vp", 0)][:, sl], odd(mE0, half * 2, 8),
                stt[("p", 0)][:, sl], ALU.subtract),
               waits=[f"ap{sfx}"], key=f"avp{sfx}")
            op("scalar", lambda e: e.activation(
                stt[("vp", 0)][:, sl], stt[("vp", 0)][:, sl], AF.Ln, bias=EPS),
               waits=[f"avp{sfx}", "ld_sm"], key=f"aln{sfx}")
            op("scalar", lambda e: e.activation(
                stt[("rs", 0)][:, sl], stt[("vp", 0)][:, sl], AF.Exp,
                scale=-0.5), waits=[f"aln{sfx}"], key=f"aexp{sfx}")
            op("vector", lambda e: e.scalar_tensor_tensor(
                stt[("nb", 0)][:, sl], even(mE0, half * 2, 8), -1.0,
                stt[("rs", 0)][:, sl], ALU.mult, ALU.mult),
               waits=[f"aexp{sfx}", f"am{sfx}"], key=f"anb{sfx}")

        for t in range(NT):
            b = t % 2
            if t == 0:
                op("tensor", transposes(0, "A"),
                   waits=["b4_0", WFT_READY], key="c1_0")
                op("tensor", transposes(1, "A"),
                   waits=["b4_1"], key="c1_1")
            elif t + 1 < NT:
                op("tensor", transposes(t + 1, "A"),
                   waits=[f"b4_{t + 1}", f"a4_{t - 1}"], key=f"c1_{t + 1}")
            if t < 2:
                if t == 0:
                    emit_stats0q_nb(1)
                emit_ln0_reduces(8 + 2 * t)
                emit_ln0_reduces(9 + 2 * t)
                emit_apply0(t + 3)
            op("scalar", (lambda b=b, t=t: lambda e: e.copy(
                xnt[:, b * D:(b + 1) * D], psT[b][:, :]))(),
               waits=[f"c1_{t}", f"c2_{t - 2}" if t >= 2 else None],
               key=f"a4_{t}")
            op("tensor", ffn_mm(t),
               waits=[f"a4_{t}", WFT_READY if t == 0 else None,
                      f"a5_{t - 2}" if t >= 2 else None],
               key=f"c2_{t}")
            op("vector", (lambda b=b, t=t: lambda e: e.scalar_tensor_tensor(
                x[:, t * D:(t + 1) * D], psF[b][:, :], alphab,
                x[:, t * D:(t + 1) * D], ALU.mult, ALU.add,
                accum_out=stt[("msss", 1)][:, 2 * t:2 * t + 1]))(),
               waits=[f"c2_{t}", "bcp"], key=f"a5_{t}")
            op("vector", (lambda t=t: lambda e: e.tensor_copy(
                x1b[:, t * D:(t + 1) * D], x[:, t * D:(t + 1) * D]))(),
               waits=[f"a5_{t}"], key=f"x1b_{t}")
            for tb in ([t - 1] if t >= 1 else []) + ([t] if t == NT - 1 else []):
                op("scalar", (lambda tb=tb: lambda e: e.activation(
                    sqs[:, :], x[:, tb * D:(tb + 1) * D], AF.Square,
                    accum_out=stt[("msss", 1)][:, 2 * tb + 1:2 * tb + 2]))(),
                   waits=[f"a5_{tb}", sqs_chain[0]], key=f"b5_{tb}")
                sqs_chain[0] = f"b5_{tb}"
            if t == 0:
                emit_stats0q_s(1)
            if 2 <= t < 4:
                emit_ln0_reduces(8 + 2 * t)
                emit_ln0_reduces(9 + 2 * t)
            if t == 4:
                emit_stats0(1)
            if t >= 2 and t + 3 < NT:
                emit_apply0(t + 3)

        msss1, mE1 = stt[("msss", 1)], stt[("mE", 1)]
        op("vector", lambda e: e.tensor_scalar(
            mE1[:, :], msss1[:, :], INVN, None, ALU.mult),
           waits=[f"a5_{NT-1}", f"b5_{NT-1}"], key="bm1")
        op("vector", lambda e: e.scalar_tensor_tensor(
            stt[("p", 1)][:, :], even(mE1, 0, NT), 1.0, even(mE1, 0, NT),
            ALU.bypass, ALU.mult),
           waits=["bm1"], key="bp1")
        op("vector", lambda e: e.tensor_tensor(
            stt[("vp", 1)][:, :], odd(mE1, 0, NT), stt[("p", 1)][:, :],
            ALU.subtract),
           waits=["bp1"], key="bvp1")
        op("scalar", lambda e: e.activation(
            stt[("vp", 1)][:, :], stt[("vp", 1)][:, :], AF.Ln, bias=EPS),
           waits=["bvp1", "ld_sm"], key="bln1")
        op("scalar", lambda e: e.activation(
            stt[("rs", 1)][:, :], stt[("vp", 1)][:, :], AF.Exp, scale=-0.5),
           waits=["bln1"], key="bexp1")
        op("vector", lambda e: e.scalar_tensor_tensor(
            rm[:, :], even(mE1, 0, NT), 1.0, stt[("rs", 1)][:, :],
            ALU.bypass, ALU.mult),
           waits=["bexp1"], key="brm")
        op("vector", lambda e: e.tensor_scalar(
            rs1b[:, :], stt[("rs", 1)][:, :], 1.0 / SEQ, None, ALU.mult),
           waits=["bexp1"], key="rs1b")

        def colsum_all(e):
            ins = None
            for t in range(NT):
                for c in range(NCH):
                    ins = e.matmul(
                        psCS[:, c:c + 1],
                        x1b[:, t * D + c * 128: t * D + (c + 1) * 128],
                        rs1b[:, t:t + 1],
                        start=(t == 0 and c == 0),
                        stop=(t == NT - 1 and c == NCH - 1),
                        skip_group_check=True)
            return ins
        op("tensor", colsum_all,
           waits=["rs1b", f"x1b_{NT-1}"], key="pe_cs")

        op("vector", lambda e: e.tensor_reduce(
            rmred[:, :], rm[:, :], AX, ALU.add),
           waits=["brm"], key="rmred")
        op("tensor", lambda e: e.matmul(
            psSm[0:1, 0:1], rmred[:, :], q4col[:, :], start=True, stop=True),
           waits=["rmred", "bcp", "q4"], key="pe_corr")
        op("vector", lambda e: e.tensor_copy(scp[0:1, 3:4], psSm[0:1, 0:1]),
           waits=["pe_corr"], key="corrsc")
        op("tensor", lambda e: e.matmul(
            psSm[:, 0:1], onesrow[0:1, :], scp[0:1, 3:4], start=True, stop=True),
           waits=["corrsc"], key="pe_corrb")
        op("vector", lambda e: e.tensor_copy(corrb, psSm[:, 0:1]),
           waits=["pe_corrb"], key="corrbc")
        op("vector", lambda e: e.tensor_scalar(
            csum[:, :], psCS[:, :], corrb, None, ALU.subtract),
           waits=["pe_cs", "corrbc"], key="evcs")
        op("gpsimd", lambda e: e.dma_start(out=ccin[:, :], in_=csum[:, :]),
           waits=["evcs"], dma="qcc", key="ccst")

        def pe_warm(e):
            ins = None
            for _ in range(120):
                ins = e.matmul(psF[0][:, 0:512], idb[:, :], wft[:, 0:512],
                               start=True, stop=True, skip_group_check=True)
            return ins
        op("tensor", pe_warm, waits=["evcs"], key="pewarm")

        def coll(e):
            ins = e.collective_compute(
                "AllGather", ALU.bypass,
                replica_groups=[[0, 1], [2, 3], [4, 5], [6, 7]],
                ins=[ccin.ap().opt()], outs=[ccout.ap().opt()])
            ins.then_inc(sems["scc"], 1)
            e.wait_ge(sems["scc"], 1)
            return e.engine_nop()
        op("gpsimd", coll, waits=["ccst"], key="coll")

        op("gpsimd", lambda e: e.dma_start(
            out=ccsb.ap().rearrange("p (r c) -> p r c", c=NCH),
            in_=ccout.ap().rearrange("(r p) c -> p r c", p=128)),
           waits=["coll"], dma="qcr", key="ccrd")
        op("vector", lambda e: e.tensor_tensor(
            meanT[:, :], ccsb[:, 0:NCH], ccsb[:, NCH:2 * NCH], ALU.add),
           waits=["ccrd"], key="pairsum")
        op("vector", lambda e: e.tensor_copy(meanTb[:, :], meanT[:, :]),
           waits=["pairsum"], key="meanb")

        def wv_mm(e):
            ins = None
            for c in range(NCH):
                for h in range(2):
                    ins = e.matmul(
                        psF[0][0:1, h * 512:(h + 1) * 512],
                        meanTb[:, c:c + 1],
                        bass.AP(wfrb, c * D + h * 512,
                                [[2 * NCH * D, 128], [1, 512]]),
                        start=(c == 0), stop=(c == NCH - 1))
            return ins
        op("tensor", wv_mm, waits=["meanb", "ld_wv", f"a5_{NT-1}"], key="pb1")
        op("vector", lambda e: e.tensor_tensor(
            vbar[:, :], psF[0][0:1, :], bv4[:, :], ALU.add),
           waits=["pb1", "ld_sm"], key="pb2")

        def vb_tp(e):
            ins = None
            for c in range(NCH):
                ins = e.transpose(
                    psF[1][:, c:c + 1],
                    vbar[0:1, c * 128:(c + 1) * 128],
                    idf[0:1, 0:1])
            return ins
        op("tensor", vb_tp, waits=["pb2", f"b5_{NT-1}", "ld_sm"], key="pb3")
        op("vector", lambda e: e.tensor_copy(vbarT[:, :], psF[1][:, 0:NCH]),
           waits=["pb3"], key="pb4")

        def wo_mm(e):
            ins = None
            for c in range(NCH):
                for h in range(2):
                    ins = e.matmul(
                        psF[0][0:1, h * 512:(h + 1) * 512],
                        vbarT[:, c:c + 1],
                        bass.AP(wfrb, NCH * D + c * D + h * 512,
                                [[2 * NCH * D, 128], [1, 512]]),
                        start=(c == 0), stop=(c == NCH - 1))
            return ins
        op("tensor", wo_mm, waits=["pb4", "ld_wo"], key="pb5")
        op("vector", lambda e: e.tensor_tensor(
            vbar[:, :], psF[0][0:1, :], bo4[:, :], ALU.add),
           waits=["pb5"], key="pb6")

        def bc_mm(e):
            ins = None
            for h in range(2):
                ins = e.matmul(
                    psF[1][:, h * 512:(h + 1) * 512],
                    onesrow[0:1, :], vbar[0:1, h * 512:(h + 1) * 512],
                    start=True, stop=True)
            return ins
        op("tensor", bc_mm, waits=["pb6", "ones2"], key="pb7")
        op("vector", lambda e: e.tensor_copy(bcast[:, :], psF[1][:, :]),
           waits=["pb7"], key="bc")
        op("vector", lambda e: e.tensor_reduce(
            bsum[:, :], bcast[:, :], AX, ALU.add),
           waits=["bc"], key="bsum")
        op("vector", lambda e: e.tensor_scalar(
            bs1024, bsum[:, :], INVN, None, ALU.mult),
           waits=["bsum"], key="bs")

        def emit_d1e1(t, eng, bcw=False):
            xt = x[:, t * D:(t + 1) * D]
            op(eng, (lambda xt=xt: lambda e: e.tensor_tensor(
                xt, bcast[:, :], xt, ALU.add))(),
               waits=["bc" if bcw else None, f"x1b_{t}", f"b5_{t}"],
               key=f"d1_{t}")
            op("scalar", (lambda xt=xt, t=t: lambda e: e.activation(
                sqs[:, :], xt, AF.Square,
                accum_out=stt[("msss", 2)][:, 2 * t + 1:2 * t + 2]))(),
               waits=[f"d1_{t}", sqs_chain[0]], key=f"e1_{t}")
            sqs_chain[0] = f"e1_{t}"

        msss2, mE2 = stt[("msss", 2)], stt[("mE", 2)]

        def ev2(tns, p, o=0):
            return bass.AP(tns, 4 * p + o, [[tns.shape[1], 128], [2, 2]])

        def emit_stats2p(p):
            sl = slice(2 * p, 2 * p + 2)
            op("vector", (lambda p=p: lambda e: e.tensor_scalar(
                ev2(mE2, p), ev2(stt[("msss", 1)], p), INVN, bs1024,
                ALU.mult, ALU.add))(),
               waits=[f"a5_{2 * p + 1}", "bs"], key=f"cm2_{p}")
            op("vector", (lambda p=p: lambda e: e.tensor_scalar(
                ev2(mE2, p, 1), ev2(msss2, p, 1), INVN, None, ALU.mult))(),
               waits=[f"e1_{2 * p + 1}"], key=f"cE2_{p}")
            op("vector", (lambda p=p, sl=sl: lambda e: e.scalar_tensor_tensor(
                stt[("p", 2)][:, sl], ev2(mE2, p), 1.0, ev2(mE2, p),
                ALU.bypass, ALU.mult))(),
               waits=[f"cm2_{p}"], key=f"cp2_{p}")
            op("vector", (lambda p=p, sl=sl: lambda e: e.tensor_tensor(
                stt[("vp", 2)][:, sl], ev2(mE2, p, 1), stt[("p", 2)][:, sl],
                ALU.subtract))(),
               waits=[f"cp2_{p}", f"cE2_{p}"], key=f"cvp2_{p}")

        def emit_stats2p_s(p):
            sl = slice(2 * p, 2 * p + 2)
            op("scalar", (lambda sl=sl: lambda e: e.activation(
                stt[("vp", 2)][:, sl], stt[("vp", 2)][:, sl], AF.Ln,
                bias=EPS))(),
               waits=[f"cvp2_{p}"], key=f"cln2_{p}")
            op("scalar", (lambda sl=sl: lambda e: e.activation(
                stt[("rs", 2)][:, sl], stt[("vp", 2)][:, sl], AF.Exp,
                scale=-0.5))(),
               waits=[f"cln2_{p}"], key=f"cexp2_{p}")

        def emit_stats2p_nb(p):
            sl = slice(2 * p, 2 * p + 2)
            op("vector", (lambda p=p, sl=sl: lambda e: e.scalar_tensor_tensor(
                stt[("nb", 2)][:, sl], ev2(mE2, p), -1.0,
                stt[("rs", 2)][:, sl], ALU.mult, ALU.mult))(),
               waits=[f"cexp2_{p}", f"cm2_{p}"], key=f"cnb2_{p}")

        def emit_stats2p_all(p):
            emit_stats2p(p)
            emit_stats2p_s(p)
            emit_stats2p_nb(p)

        def emit_apply2(t):
            b4_ = t % 4
            ws = [f"cnb2_{t // 2}", f"d1_{t}",
                  f"f1_{t - 4}" if t >= 4 else f"c1_{NT - 4 + t}"]
            if t % 2 == 0:
                op("scalar", (lambda t=t, b4_=b4_: lambda e: e.activation(
                    xn[:, b4_ * D:(b4_ + 1) * D], x[:, t * D:(t + 1) * D],
                    AF.Identity, bias=stt[("nb", 2)][:, t:t + 1],
                    scale=stt[("rs", 2)][:, t:t + 1]))(),
                   waits=ws, key=f"e4_{t}")
            else:
                op("vector", (lambda t=t, b4_=b4_: lambda e: e.tensor_scalar(
                    xn[:, b4_ * D:(b4_ + 1) * D], x[:, t * D:(t + 1) * D],
                    stt[("rs", 2)][:, t:t + 1], stt[("nb", 2)][:, t:t + 1],
                    ALU.mult, ALU.add))(),
                   waits=ws, key=f"e4_{t}")

        emit_d1e1(0, "vector", bcw=True)
        emit_d1e1(1, "vector")
        emit_stats2p_all(0)
        emit_apply2(0)
        emit_apply2(1)
        for t in range(NT):
            b = t % 2
            if t == 0:
                op("tensor", transposes(0, "C"),
                   waits=["e4_0", f"a4_{NT-2}"], key="f1_0")
                op("tensor", transposes(1, "C"),
                   waits=["e4_1", f"a4_{NT-1}"], key="f1_1")
            elif t + 1 < NT:
                op("tensor", transposes(t + 1, "C"),
                   waits=[f"e4_{t + 1}", f"d4_{t - 1}"], key=f"f1_{t + 1}")
            op("scalar", (lambda b=b: lambda e: e.copy(
                xnt[:, b * D:(b + 1) * D], psT[b][:, :]))(),
               waits=[f"f1_{t}",
                      f"f2_{t - 2}" if t >= 2 else f"c2_{NT - 2 + t}"],
               key=f"d4_{t}")
            op("tensor", ffn_mm(t),
               waits=[f"d4_{t}", f"d5_{t - 2}" if t >= 2 else "bc"],
               key=f"f2_{t}")
            for td in ([t - 1] if t >= 1 else []) + ([t] if t == NT - 1 else []):
                bd = td % 2
                op("vector", (lambda bd=bd, td=td: lambda e: e.scalar_tensor_tensor(
                    x[:, td * D:(td + 1) * D], psF[bd][:, :], alphab,
                    x[:, td * D:(td + 1) * D], ALU.mult, ALU.add))(),
                   waits=[f"f2_{td}", "bcp"], key=f"d5_{td}")
                op("sync", (lambda td=td: lambda e: e.dma_start(
                    out=out_h[td * 128:(td + 1) * 128, :],
                    in_=x[:, td * D:(td + 1) * D]))(),
                   waits=[f"d5_{td}"], dma="qy", key=f"st_{td}")
            if t == 0:
                emit_d1e1(2, "vector")
                emit_d1e1(3, "vector")
                emit_stats2p_all(1)
                emit_apply2(2)
            elif 2 * t + 3 < NT:
                emit_d1e1(2 * t + 2, "vector")
                emit_d1e1(2 * t + 3, "vector")
                emit_stats2p_all(t + 1)
            if t + 3 < NT:
                emit_apply2(t + 3)

        def emit_for(name):
            def body(eng):
                for (en, waits, fn, sem, amt) in sched:
                    if en != name:
                        continue
                    for w in waits:
                        if isinstance(w, str):
                            w = land[w]
                        eng.wait_ge(sems[w[0]], w[1])
                    ins = fn(eng)
                    if sem is not None:
                        ins.then_inc(sems[sem], amt)
                if name == "sync":
                    eng.wait_ge(sems["qy"], 16 * NT)
            return body

        block.sync(emit_for("sync"))
        block.gpsimd(emit_for("gpsimd"))
        block.vector(emit_for("vector"))
        block.scalar(emit_for("scalar"))
        block.tensor(emit_for("tensor"))

    return nc


def kernel(L0_attn_x, L0_attn_wq, L0_attn_bq, L0_attn_wk, L0_attn_bk,
           L0_attn_wv, L0_attn_bv, L0_attn_wo, L0_attn_bo, L0_attn_w1,
           L0_attn_b1, L0_attn_w2, L0_attn_b2, L0_attn_hadamard, L0_attn_rot,
           L0_ffn_weight):
    assert L0_attn_wq.shape == L0_attn_wk.shape == (D, D)
    assert L0_attn_bq.shape == L0_attn_bk.shape == (D,)
    assert L0_attn_w1.shape == (64, D) and L0_attn_b1.shape == (D,)
    assert L0_attn_w2.shape == (D, 16) and L0_attn_b2.shape == (16,)
    assert L0_attn_hadamard.shape == (8, 8) and L0_attn_rot.shape == (16, 8, 3)

    x = np.ascontiguousarray(np.asarray(L0_attn_x, dtype=np.float32))
    wf = np.ascontiguousarray(np.asarray(L0_ffn_weight, dtype=np.float32))
    wv = np.ascontiguousarray(np.asarray(L0_attn_wv, dtype=np.float32))
    wo = np.ascontiguousarray(np.asarray(L0_attn_wo, dtype=np.float32))
    bv = np.ascontiguousarray(np.asarray(L0_attn_bv, dtype=np.float32)).reshape(1, D)
    bo = np.ascontiguousarray(np.asarray(L0_attn_bo, dtype=np.float32)).reshape(1, D)
    B, S, _ = x.shape

    if "nc" not in _CACHE:
        _CACHE["nc"] = _build()
    nc = _CACHE["nc"]

    idb = np.eye(128, dtype=ml_dtypes.bfloat16)
    idf = np.eye(4, dtype=np.float32)
    cst = np.zeros((128, 2), dtype=np.float32)
    cst[:, 0] = EPS
    cst[:, 1] = MAGIC
    in_maps = []
    for c in range(8):
        b, h = c // 2, c % 2
        in_maps.append({
            "x": np.ascontiguousarray(x[b, h * TOKS:(h + 1) * TOKS, :]),
            "wf": wf, "wv": wv, "wo": wo, "bv": bv, "bo": bo,
            "idb": idb, "idf": idf, "cst": cst,
        })

    res = run_bass_kernel_spmd(nc, in_maps, core_ids=list(range(8)))

    out = np.empty((B, S, D), dtype=np.float32)
    for c in range(8):
        b, h = c // 2, c % 2
        out[b, h * TOKS:(h + 1) * TOKS, :] = res.results[c]["out"]
    return out


# version 30
# speedup vs baseline: 1.0102x; 1.0102x over previous
import numpy as np
import ml_dtypes
from contextlib import ExitStack

import concourse.bass as bass
import concourse.mybir as mybir
from concourse.bass_utils import run_bass_kernel_spmd

F32 = mybir.dt.float32
BF16 = mybir.dt.bfloat16
AX = mybir.AxisListType.X
ALU = mybir.AluOpType
AF = mybir.ActivationFunctionType

MAGIC = 12582912.0
D = 1024
NCH = 8
NT = 16
TOKS = NT * 128
SEQ = 4096
EPS = 1e-5
INVN = 1.0 / D

_CACHE = {}


def _build():
    nc = bass.Bass("TRN2", target_bir_lowering=False)

    x_h = nc.dram_tensor("x", [TOKS, D], F32, kind="ExternalInput")
    wf_h = nc.dram_tensor("wf", [D, D], F32, kind="ExternalInput")
    wv_h = nc.dram_tensor("wv", [D, D], F32, kind="ExternalInput")
    wo_h = nc.dram_tensor("wo", [D, D], F32, kind="ExternalInput")
    bv_h = nc.dram_tensor("bv", [1, D], F32, kind="ExternalInput")
    bo_h = nc.dram_tensor("bo", [1, D], F32, kind="ExternalInput")
    ccin = nc.dram_tensor("ccin", [128, NCH], F32, kind="Internal")
    ccout = nc.dram_tensor("ccout", [2 * 128, NCH], F32, kind="Internal")
    idb_h = nc.dram_tensor("idb", [128, 128], BF16, kind="ExternalInput")
    idf_h = nc.dram_tensor("idf", [4, 4], F32, kind="ExternalInput")
    cst_h = nc.dram_tensor("cst", [128, 2], F32, kind="ExternalInput")
    out_h = nc.dram_tensor("out", [TOKS, D], F32, kind="ExternalOutput")

    with ExitStack() as st:
        sb = lambda name, shape, dt: st.enter_context(nc.sbuf_tensor(name, shape, dt))
        ps = lambda name, shape, dt: st.enter_context(nc.psum_tensor(name, shape, dt))

        x = sb("xres", [128, NT * D], F32)
        x1b = sb("x1b", [128, NT * D], BF16)
        wft = sb("wft", [128, NCH * D], BF16)
        wfr = sb("wfr", [128, NCH * D], F32)
        wfrb = wfr.bitcast(BF16)
        qbf = sb("qbf", [128, 2 * D], F32)
        wtc = sb("wtc", [128, 2 * D], BF16)
        xn = sb("xn", [128, 4 * D], BF16)
        xnt = sb("xnt", [128, 2 * D], BF16)
        sqs = sb("sqs", [128, D], F32)
        bcast = sb("bcast", [128, D], F32)
        ccsb = sb("ccsb", [128, 2 * NCH], F32)
        meanT = sb("meanT", [128, NCH], F32)
        meanTb = sb("meanTb", [128, NCH], BF16)
        csum = sb("csum", [128, NCH], F32)
        vbar = sb("vbar", [1, D], F32)
        vbarT = sb("vbarT", [128, NCH], BF16)
        bv4 = sb("bv4", [1, D], F32)
        bo4 = sb("bo4", [1, D], F32)
        idb = sb("idb_s", [128, 128], BF16)
        idf = sb("idf_s", [4, 4], F32)
        cst = sb("cst_s", [128, 2], F32)
        onescol = sb("onescol", [128, 1], F32)
        q4col = sb("q4col", [128, 1], F32)
        onesrow = sb("onesrow", [1, 128], F32)
        scp = sb("scp", [1, 4], F32)
        bcp = sb("bcp", [128, 4], F32)
        absac = sb("absac", [128, NCH], F32)
        abscol = sb("abscol", [128, 1], F32)
        rmred = sb("rmred", [128, 1], F32)
        bsum = sb("bsum", [128, 2], F32)
        stt = {}
        for stg in (0, 1, 2):
            stt[("msss", stg)] = sb(f"st_msss{stg}", [128, 2 * NT], F32)
            stt[("mE", stg)] = sb(f"st_mE{stg}", [128, 2 * NT], F32)
            stt[("p", stg)] = sb(f"st_p{stg}", [128, NT], F32)
            stt[("vp", stg)] = sb(f"st_vp{stg}", [128, NT], F32)
            stt[("rs", stg)] = sb(f"st_rs{stg}", [128, NT], F32)
        for stg in (0, 2):
            stt[("nb", stg)] = sb(f"st_nb{stg}", [128, NT], F32)
        rm = sb("st_rm", [128, NT], F32)
        rs1b = sb("rs1b", [128, NT], BF16)

        psF = [ps("psF0", [128, D], F32), ps("psF1", [128, D], F32)]
        psT = [ps("psT0", [128, D], BF16), ps("psT1", [128, D], BF16)]
        psCS = ps("psCS", [128, NCH], F32)
        psSm = ps("psSm", [128, 2], F32)

        nc.const_aps.register(F32, EPS, cst[:, 0:1])
        nc.const_aps.register(F32, MAGIC, cst[:, 1:2])

        sems = {}
        semnames = (["v", "s", "pe", "pl", "scc", "qsm", "qwv", "qwo",
                     "qcc", "qcr", "qy"]
                    + [f"qx{g}" for g in range(4)]
                    + [f"qw{q}" for q in range(4)])
        for nm in semnames:
            sems[nm] = st.enter_context(nc.semaphore(nm))

        block = st.enter_context(nc.Block())

        sched = []
        cnt = {k: 0 for k in sems}
        land = {}
        sqs_chain = [None]

        def emit_ln0_reduces(t):
            xt = x[:, t * D:(t + 1) * D]
            if f"a1_{t}" not in land:
                op("vector", (lambda xt=xt, t=t: lambda e: e.tensor_reduce(
                    stt[("msss", 0)][:, 2 * t:2 * t + 1], xt, AX, ALU.add))(),
                   waits=[f"ld_xg{t // 4}"], key=f"a1_{t}")
            op("scalar", (lambda xt=xt, t=t: lambda e: e.activation(
                sqs[:, :], xt, AF.Square,
                accum_out=stt[("msss", 0)][:, 2 * t + 1:2 * t + 2]))(),
               waits=[f"ld_xg{t // 4}", sqs_chain[0]], key=f"b1_{t}")
            sqs_chain[0] = f"b1_{t}"
        ENG_SEM = {"vector": "v", "scalar": "s", "tensor": "pe", "gpsimd": "pl",
                   "sync": "qy"}

        def op(eng, fn, waits=(), key=None, dma=None, noinc=False):
            ws = [w for w in waits if w is not None]
            if noinc:
                sched.append((eng, ws, fn, None, 0))
                return None
            sem = dma if dma else ENG_SEM[eng]
            amt = 16 if dma else 1
            sched.append((eng, ws, fn, sem, amt))
            cnt[sem] += amt
            if key:
                land[key] = (sem, cnt[sem])
            return (sem, cnt[sem])

        for pair in range(4):
            g2 = 2 * pair
            op("sync" if pair < 2 else "scalar",
               (lambda g2=g2: lambda e: e.dma_start(
                   out=wfr.ap()[:, g2 * D:(g2 + 2) * D]
                        .rearrange("p (c d) -> p c d", d=D),
                   in_=wf_h.ap()[g2 * 128:(g2 + 2) * 128, :]
                        .rearrange("(c p) d -> p c d", p=128)))(),
               dma=f"qw{pair}", key=f"wfq{pair}")
        op("scalar", lambda e: e.dma_start(out=idb[:, :], in_=idb_h[:, :]),
           dma="qsm")
        op("scalar", lambda e: e.dma_start(out=cst[:, :], in_=cst_h[:, :]),
           dma="qsm")
        op("scalar", lambda e: e.dma_start(out=idf[:, :], in_=idf_h[:, :]),
           dma="qsm")
        op("scalar", lambda e: e.dma_start(out=bv4[0:1, :], in_=bv_h[0:1, :]),
           dma="qsm")
        op("scalar", lambda e: e.dma_start(out=bo4[0:1, :], in_=bo_h[0:1, :]),
           dma="qsm", key="ld_sm")
        def emit_xg(g, eng):
            op(eng,
               (lambda gg: lambda e: e.dma_start(
                   out=x.ap()[:, gg * 4 * D:(gg + 1) * 4 * D]
                        .rearrange("p (t d) -> p t d", d=D),
                   in_=x_h.ap()[gg * 512:(gg + 1) * 512, :]
                        .rearrange("(t p) d -> p t d", p=128)))(g),
               dma=f"qx{g}", key=f"ld_xg{g}")
        emit_xg(0, "sync")
        emit_xg(1, "sync")
        emit_xg(2, "scalar")
        emit_xg(3, "sync")

        op("gpsimd", lambda e: e.dma_start(
            out=bass.AP(wfrb, 0, [[2 * NCH * D, 128], [D, NCH], [1, D]]),
            in_=wv_h.ap().rearrange("(c p) o -> p c o", p=128)),
           waits=["sq1_7"], dma="qwv", key="ld_wv")
        op("gpsimd", lambda e: e.dma_start(
            out=bass.AP(wfrb, NCH * D, [[2 * NCH * D, 128], [D, NCH], [1, D]]),
            in_=wo_h.ap().rearrange("(c p) o -> p c o", p=128)),
           waits=[("qwv", 16)], dma="qwo", key="ld_wo")

        op("vector", lambda e: e.memset(onescol[:, :], 1.0), key="ones")
        op("vector", lambda e: e.memset(q4col[:, :], 1.0 / SEQ), key="q4")
        op("vector", lambda e: e.memset(onesrow[:, :], 1.0), key="ones2")

        def emit_a1(t):
            op("vector", (lambda t=t: lambda e: e.tensor_reduce(
                stt[("msss", 0)][:, 2 * t:2 * t + 1],
                x[:, t * D:(t + 1) * D], AX, ALU.add))(),
               waits=[f"ld_xg{t // 4}"], key=f"a1_{t}")

        def emit_b1(t):
            op("scalar", (lambda t=t: lambda e: e.activation(
                sqs[:, :], x[:, t * D:(t + 1) * D], AF.Square,
                accum_out=stt[("msss", 0)][:, 2 * t + 1:2 * t + 2]))(),
               waits=[f"ld_xg{t // 4}", sqs_chain[0]], key=f"b1_{t}")
            sqs_chain[0] = f"b1_{t}"

        for c in (0, 4, 1, 5, 2, 6, 3, 7):
            op("scalar", (lambda cc: lambda e: e.activation(
                sqs[:, :], wfr[:, cc * D:(cc + 1) * D], AF.Abs,
                accum_out=absac[:, cc:cc + 1]))(c),
               waits=[f"wfq{c // 2}", sqs_chain[0]],
               key=f"abs{c}")
            sqs_chain[0] = f"abs{c}"
        op("vector", lambda e: e.tensor_reduce(
            abscol[:, :], absac[:, :], AX, ALU.add),
           waits=["abs7"], key="abscol")
        op("tensor", lambda e: e.matmul(
            psSm[0:1, 0:1], abscol[:, :], onescol[:, :], start=True, stop=True),
           waits=["abscol", "ones"], key="pe_sc")
        op("vector", lambda e: e.tensor_scalar(
            scp[0:1, 0:1], psSm[0:1, 0:1], 1.0 / (D * D), EPS, ALU.mult, ALU.max),
           waits=["pe_sc"], key="sc0")
        op("vector", lambda e: e.reciprocal(scp[0:1, 1:2], scp[0:1, 0:1]),
           waits=["sc0"], key="invk")
        op("vector", lambda e: e.tensor_scalar(
            scp[0:1, 2:3], scp[0:1, 0:1], 0.5, None, ALU.mult),
           waits=["sc0"], key="hs")
        op("tensor", lambda e: e.matmul(
            psSm[:, 0:2], onesrow[0:1, :], scp[0:1, 1:3], start=True, stop=True),
           waits=["hs", "invk", "ones2"], key="pe_bc")
        op("vector", lambda e: e.tensor_copy(bcp[:, 0:2], psSm[:, :]),
           waits=["pe_bc"], key="bcp")
        invb = bcp[:, 0:1]
        alphab = bcp[:, 1:2]
        corrb = bcp[:, 2:3]
        bs1024 = bcp[:, 3:4]

        def even(tns, g, n=4):
            return bass.AP(tns, 8 * g, [[tns.shape[1], 128], [2, n]])

        def odd(tns, g, n=4):
            return bass.AP(tns, 8 * g + 1, [[tns.shape[1], 128], [2, n]])

        msss0, mE0 = stt[("msss", 0)], stt[("mE", 0)]

        def emit_stats0q(q):
            sl = slice(4 * q, 4 * q + 4)
            op("vector", (lambda q=q: lambda e: e.tensor_scalar(
                mE0[:, 8 * q:8 * q + 8], msss0[:, 8 * q:8 * q + 8],
                INVN, None, ALU.mult))(),
               waits=[f"a1_{4 * q + 3}", f"b1_{4 * q + 3}"], key=f"amA{q}")
            op("vector", (lambda q=q, sl=sl: lambda e: e.scalar_tensor_tensor(
                stt[("p", 0)][:, sl], even(mE0, q), 1.0, even(mE0, q),
                ALU.bypass, ALU.mult))(),
               waits=[f"amA{q}"], key=f"apA{q}")
            op("vector", (lambda q=q, sl=sl: lambda e: e.tensor_tensor(
                stt[("vp", 0)][:, sl], odd(mE0, q), stt[("p", 0)][:, sl],
                ALU.subtract))(),
               waits=[f"apA{q}"], key=f"avpA{q}")

        def emit_stats0q_s(q):
            sl = slice(4 * q, 4 * q + 4)
            op("scalar", (lambda sl=sl: lambda e: e.activation(
                stt[("vp", 0)][:, sl], stt[("vp", 0)][:, sl], AF.Ln,
                bias=EPS))(),
               waits=[f"avpA{q}", "ld_sm"], key=f"alnA{q}")
            op("scalar", (lambda sl=sl: lambda e: e.activation(
                stt[("rs", 0)][:, sl], stt[("vp", 0)][:, sl], AF.Exp,
                scale=-0.5))(),
               waits=[f"alnA{q}"], key=f"aexpA{q}")

        def emit_stats0q_nb(q):
            sl = slice(4 * q, 4 * q + 4)
            op("vector", (lambda q=q, sl=sl: lambda e: e.scalar_tensor_tensor(
                stt[("nb", 0)][:, sl], even(mE0, q), -1.0,
                stt[("rs", 0)][:, sl], ALU.mult, ALU.mult))(),
               waits=[f"aexpA{q}", f"amA{q}"], key=f"anbA{q}")

        def emit_apply0(t):
            b4_ = t % 4
            op("vector", (lambda t=t, b4_=b4_: lambda e: e.tensor_scalar(
                xn[:, b4_ * D:(b4_ + 1) * D], x[:, t * D:(t + 1) * D],
                stt[("rs", 0)][:, t:t + 1], stt[("nb", 0)][:, t:t + 1],
                ALU.mult, ALU.add))(),
               waits=["anbA0" if t < 4 else ("anbA1" if t < 8 else "anbB"),
                      f"c1_{t - 4}" if t >= 4 else None],
               key=f"b4_{t}")

        def emit_twe(c):
            op("vector", (lambda cc=c: lambda e: e.tensor_copy(
                bass.AP(wft, cc * 128,
                        [[wft.shape[1], 128], [1024, NCH], [1, 128]]),
                bass.AP(psT[0], 0,
                        [[psT[0].shape[1], 128], [128, NCH], [1, 128]])))(),
               waits=[f"tw{c}"], key=f"twe{c}")

        for c in range(NCH):
            b2 = c % 2
            op("scalar", (lambda cc, b2=b2: lambda e: e.activation(
                qbf[:, b2 * D:(b2 + 1) * D], wfr[:, cc * D:(cc + 1) * D],
                AF.Identity, bias=MAGIC, scale=invb))(c),
               waits=["bcp", "ld_sm",
                      f"q3_{c - 2}" if c >= 2 else None],
               key=f"sq1_{c}")
            op("vector", (lambda b2=b2: lambda e: e.tensor_scalar(
                qbf[:, b2 * D:(b2 + 1) * D], qbf[:, b2 * D:(b2 + 1) * D],
                -MAGIC, -1.0, ALU.add, ALU.max))(),
               waits=[f"sq1_{c}"], key=f"q2_{c}")
            op("vector", (lambda b2=b2: lambda e: e.tensor_scalar(
                wtc[:, b2 * D:(b2 + 1) * D], qbf[:, b2 * D:(b2 + 1) * D],
                1.0, None, ALU.min))(),
               waits=[f"q2_{c}", f"tw{c - 2}" if c >= 2 else None],
               key=f"q3_{c}")
            if c >= 1:
                emit_twe(c - 1)
            if c == 1:
                emit_a1(0)
                emit_a1(1)
            elif c == 2:
                emit_a1(2)
                emit_a1(3)
            elif c == 3:
                emit_stats0q(0)
            elif c == 4:
                emit_stats0q_nb(0)
                emit_apply0(0)
            elif c == 5:
                emit_apply0(1)
                emit_apply0(2)
            elif c == 6:
                emit_a1(4)
            elif c == 7:
                emit_a1(5)

            def tw(cc, bb):
                def f(e):
                    ins = None
                    for k in range(NCH):
                        ins = e.transpose(
                            psT[0][:, k * 128:(k + 1) * 128],
                            wtc[:, bb * D + k * 128: bb * D + (k + 1) * 128],
                            idb[:, :])
                    return ins
                return f
            op("tensor", tw(c, b2),
               waits=[f"q3_{c}", "ld_sm", f"twe{c - 1}" if c >= 1 else None],
               key=f"tw{c}")

            if c == 1:
                emit_b1(0)
            elif c == 2:
                emit_b1(1)
                emit_b1(2)
            elif c == 3:
                emit_b1(3)
            elif c == 4:
                emit_b1(4)
                emit_stats0q_s(0)
            elif c == 5:
                emit_b1(5)
            elif c == 6:
                emit_b1(6)
        emit_twe(NCH - 1)
        WFT_READY = f"twe{NCH - 1}"
        emit_a1(6)
        emit_a1(7)
        emit_b1(7)
        emit_stats0q(1)

        def transposes(t, phase):
            b2 = t % 2
            b4_ = t % 4

            def f(e):
                ins = None
                for c in range(NCH):
                    ins = e.transpose(
                        psT[b2][:, c * 128:(c + 1) * 128],
                        xn[:, b4_ * D + c * 128: b4_ * D + (c + 1) * 128],
                        idb[:, :])
                return ins
            return f

        def ffn_mm(t):
            b = t % 2

            def f(e):
                ins = None
                for c in range(NCH):
                    for h in range(2):
                        ins = e.matmul(
                            psF[b][:, h * 512:(h + 1) * 512],
                            xnt[:, b * D + c * 128: b * D + (c + 1) * 128],
                            wft[:, c * D + h * 512: c * D + h * 512 + 512],
                            start=(c == 0), stop=(c == NCH - 1))
                return ins
            return f

        def emit_stats0(half):
            h8 = 8 * half
            sfx = "A" if half == 0 else "B"
            sl = slice(h8, h8 + 8)
            op("vector", lambda e: e.tensor_scalar(
                mE0[:, 2 * h8:2 * h8 + 16], msss0[:, 2 * h8:2 * h8 + 16],
                INVN, None, ALU.mult),
               waits=[f"a1_{h8 + 7}", f"b1_{h8 + 7}"], key=f"am{sfx}")
            op("vector", lambda e: e.scalar_tensor_tensor(
                stt[("p", 0)][:, sl], even(mE0, half * 2, 8), 1.0,
                even(mE0, half * 2, 8), ALU.bypass, ALU.mult),
               waits=[f"am{sfx}"], key=f"ap{sfx}")
            op("vector", lambda e: e.tensor_tensor(
                stt[("vp", 0)][:, sl], odd(mE0, half * 2, 8),
                stt[("p", 0)][:, sl], ALU.subtract),
               waits=[f"ap{sfx}"], key=f"avp{sfx}")
            op("scalar", lambda e: e.activation(
                stt[("vp", 0)][:, sl], stt[("vp", 0)][:, sl], AF.Ln, bias=EPS),
               waits=[f"avp{sfx}", "ld_sm"], key=f"aln{sfx}")
            op("scalar", lambda e: e.activation(
                stt[("rs", 0)][:, sl], stt[("vp", 0)][:, sl], AF.Exp,
                scale=-0.5), waits=[f"aln{sfx}"], key=f"aexp{sfx}")
            op("vector", lambda e: e.scalar_tensor_tensor(
                stt[("nb", 0)][:, sl], even(mE0, half * 2, 8), -1.0,
                stt[("rs", 0)][:, sl], ALU.mult, ALU.mult),
               waits=[f"aexp{sfx}", f"am{sfx}"], key=f"anb{sfx}")

        for t in range(NT):
            b = t % 2
            if t == 0:
                op("tensor", transposes(0, "A"),
                   waits=["b4_0", WFT_READY], key="c1_0")
                op("tensor", transposes(1, "A"),
                   waits=["b4_1"], key="c1_1")
            elif t + 1 < NT:
                op("tensor", transposes(t + 1, "A"),
                   waits=[f"b4_{t + 1}", f"a4_{t - 1}"], key=f"c1_{t + 1}")
            if t < 2:
                if t == 0:
                    emit_stats0q_nb(1)
                emit_ln0_reduces(8 + 2 * t)
                emit_ln0_reduces(9 + 2 * t)
                emit_apply0(t + 3)
            op("scalar", (lambda b=b, t=t: lambda e: e.copy(
                xnt[:, b * D:(b + 1) * D], psT[b][:, :]))(),
               waits=[f"c1_{t}", f"c2_{t - 2}" if t >= 2 else None],
               key=f"a4_{t}")
            op("tensor", ffn_mm(t),
               waits=[f"a4_{t}", WFT_READY if t == 0 else None,
                      f"a5_{t - 2}" if t >= 2 else None],
               key=f"c2_{t}")
            op("vector", (lambda b=b, t=t: lambda e: e.scalar_tensor_tensor(
                x[:, t * D:(t + 1) * D], psF[b][:, :], alphab,
                x[:, t * D:(t + 1) * D], ALU.mult, ALU.add,
                accum_out=stt[("msss", 1)][:, 2 * t:2 * t + 1]))(),
               waits=[f"c2_{t}", "bcp"], key=f"a5_{t}")
            for tb in ([t - 1] if t >= 1 else []) + ([t] if t == NT - 1 else []):
                op("scalar", (lambda tb=tb: lambda e: e.activation(
                    sqs[:, :], x[:, tb * D:(tb + 1) * D], AF.Square,
                    accum_out=stt[("msss", 1)][:, 2 * tb + 1:2 * tb + 2]))(),
                   waits=[f"a5_{tb}", sqs_chain[0]], key=f"b5_{tb}")
                sqs_chain[0] = f"b5_{tb}"
            if t == 0:
                emit_stats0q_s(1)
            if 2 <= t < 4:
                emit_ln0_reduces(8 + 2 * t)
                emit_ln0_reduces(9 + 2 * t)
            if t == 4:
                emit_stats0(1)
            if t >= 2 and t + 3 < NT:
                emit_apply0(t + 3)
            op("vector", (lambda t=t: lambda e: e.tensor_copy(
                x1b[:, t * D:(t + 1) * D], x[:, t * D:(t + 1) * D]))(),
               waits=[f"a5_{t}"], key=f"x1b_{t}")

        msss1, mE1 = stt[("msss", 1)], stt[("mE", 1)]
        op("vector", lambda e: e.tensor_scalar(
            mE1[:, :], msss1[:, :], INVN, None, ALU.mult),
           waits=[f"a5_{NT-1}", f"b5_{NT-1}"], key="bm1")
        op("vector", lambda e: e.scalar_tensor_tensor(
            stt[("p", 1)][:, :], even(mE1, 0, NT), 1.0, even(mE1, 0, NT),
            ALU.bypass, ALU.mult),
           waits=["bm1"], key="bp1")
        op("vector", lambda e: e.tensor_tensor(
            stt[("vp", 1)][:, :], odd(mE1, 0, NT), stt[("p", 1)][:, :],
            ALU.subtract),
           waits=["bp1"], key="bvp1")
        op("scalar", lambda e: e.activation(
            stt[("vp", 1)][:, :], stt[("vp", 1)][:, :], AF.Ln, bias=EPS),
           waits=["bvp1", "ld_sm"], key="bln1")
        op("scalar", lambda e: e.activation(
            stt[("rs", 1)][:, :], stt[("vp", 1)][:, :], AF.Exp, scale=-0.5),
           waits=["bln1"], key="bexp1")
        op("vector", lambda e: e.scalar_tensor_tensor(
            rm[:, :], even(mE1, 0, NT), 1.0, stt[("rs", 1)][:, :],
            ALU.bypass, ALU.mult),
           waits=["bexp1"], key="brm")
        op("vector", lambda e: e.tensor_scalar(
            rs1b[:, :], stt[("rs", 1)][:, :], 1.0 / SEQ, None, ALU.mult),
           waits=["bexp1"], key="rs1b")

        def colsum_all(e):
            ins = None
            for t in range(NT):
                for c in range(NCH):
                    ins = e.matmul(
                        psCS[:, c:c + 1],
                        x1b[:, t * D + c * 128: t * D + (c + 1) * 128],
                        rs1b[:, t:t + 1],
                        start=(t == 0 and c == 0),
                        stop=(t == NT - 1 and c == NCH - 1),
                        skip_group_check=True)
            return ins
        op("tensor", colsum_all,
           waits=["rs1b", f"x1b_{NT-1}"], key="pe_cs")

        op("vector", lambda e: e.tensor_reduce(
            rmred[:, :], rm[:, :], AX, ALU.add),
           waits=["brm"], key="rmred")
        op("tensor", lambda e: e.matmul(
            psSm[0:1, 0:1], rmred[:, :], q4col[:, :], start=True, stop=True),
           waits=["rmred", "bcp", "q4"], key="pe_corr")
        op("vector", lambda e: e.tensor_copy(scp[0:1, 3:4], psSm[0:1, 0:1]),
           waits=["pe_corr"], key="corrsc")
        op("tensor", lambda e: e.matmul(
            psSm[:, 0:1], onesrow[0:1, :], scp[0:1, 3:4], start=True, stop=True),
           waits=["corrsc"], key="pe_corrb")
        op("vector", lambda e: e.tensor_copy(corrb, psSm[:, 0:1]),
           waits=["pe_corrb"], key="corrbc")
        op("vector", lambda e: e.tensor_scalar(
            csum[:, :], psCS[:, :], corrb, None, ALU.subtract),
           waits=["pe_cs", "corrbc"], key="evcs")
        op("gpsimd", lambda e: e.dma_start(out=ccin[:, :], in_=csum[:, :]),
           waits=["evcs"], dma="qcc", key="ccst")

        def pe_warm(e):
            ins = None
            for _ in range(120):
                ins = e.matmul(psF[0][:, 0:512], idb[:, :], wft[:, 0:512],
                               start=True, stop=True, skip_group_check=True)
            return ins
        op("tensor", pe_warm, waits=["evcs"], key="pewarm")

        def coll(e):
            ins = e.collective_compute(
                "AllGather", ALU.bypass,
                replica_groups=[[0, 1], [2, 3], [4, 5], [6, 7]],
                ins=[ccin.ap().opt()], outs=[ccout.ap().opt()])
            ins.then_inc(sems["scc"], 1)
            e.wait_ge(sems["scc"], 1)
            return e.engine_nop()
        op("gpsimd", coll, waits=["ccst"], key="coll")

        op("gpsimd", lambda e: e.dma_start(
            out=ccsb.ap().rearrange("p (r c) -> p r c", c=NCH),
            in_=ccout.ap().rearrange("(r p) c -> p r c", p=128)),
           waits=["coll"], dma="qcr", key="ccrd")
        op("vector", lambda e: e.tensor_tensor(
            meanT[:, :], ccsb[:, 0:NCH], ccsb[:, NCH:2 * NCH], ALU.add),
           waits=["ccrd"], key="pairsum")
        op("vector", lambda e: e.tensor_copy(meanTb[:, :], meanT[:, :]),
           waits=["pairsum"], key="meanb")

        def wv_mm(e):
            ins = None
            for c in range(NCH):
                for h in range(2):
                    ins = e.matmul(
                        psF[0][0:1, h * 512:(h + 1) * 512],
                        meanTb[:, c:c + 1],
                        bass.AP(wfrb, c * D + h * 512,
                                [[2 * NCH * D, 128], [1, 512]]),
                        start=(c == 0), stop=(c == NCH - 1))
            return ins
        op("tensor", wv_mm, waits=["meanb", "ld_wv", f"a5_{NT-1}"], key="pb1")
        op("vector", lambda e: e.tensor_tensor(
            vbar[:, :], psF[0][0:1, :], bv4[:, :], ALU.add),
           waits=["pb1", "ld_sm"], key="pb2")

        def vb_tp(e):
            ins = None
            for c in range(NCH):
                ins = e.transpose(
                    psF[1][:, c:c + 1],
                    vbar[0:1, c * 128:(c + 1) * 128],
                    idf[0:1, 0:1])
            return ins
        op("tensor", vb_tp, waits=["pb2", f"b5_{NT-1}", "ld_sm"], key="pb3")
        op("vector", lambda e: e.tensor_copy(vbarT[:, :], psF[1][:, 0:NCH]),
           waits=["pb3"], key="pb4")

        def wo_mm(e):
            ins = None
            for c in range(NCH):
                for h in range(2):
                    ins = e.matmul(
                        psF[0][0:1, h * 512:(h + 1) * 512],
                        vbarT[:, c:c + 1],
                        bass.AP(wfrb, NCH * D + c * D + h * 512,
                                [[2 * NCH * D, 128], [1, 512]]),
                        start=(c == 0), stop=(c == NCH - 1))
            return ins
        op("tensor", wo_mm, waits=["pb4", "ld_wo"], key="pb5")
        op("vector", lambda e: e.tensor_tensor(
            vbar[:, :], psF[0][0:1, :], bo4[:, :], ALU.add),
           waits=["pb5"], key="pb6")

        def bc_mm(e):
            ins = None
            for h in range(2):
                ins = e.matmul(
                    psF[1][:, h * 512:(h + 1) * 512],
                    onesrow[0:1, :], vbar[0:1, h * 512:(h + 1) * 512],
                    start=True, stop=True)
            return ins
        op("tensor", bc_mm, waits=["pb6", "ones2"], key="pb7")
        op("vector", lambda e: e.tensor_reduce(
            bsum[0:1, 0:1], vbar[0:1, :], AX, ALU.add),
           waits=["pb6"], key="rbs")
        op("vector", lambda e: e.tensor_scalar(
            bsum[0:1, 1:2], bsum[0:1, 0:1], INVN, None, ALU.mult),
           waits=["rbs"], key="rbs2")
        op("tensor", lambda e: e.matmul(
            psSm[:, 0:1], onesrow[0:1, :], bsum[0:1, 1:2],
            start=True, stop=True),
           waits=["rbs2"], key="pe_bs")
        op("vector", lambda e: e.tensor_copy(bcast[:, :], psF[1][:, :]),
           waits=["pb7"], key="bc")
        op("vector", lambda e: e.tensor_copy(bs1024, psSm[:, 0:1]),
           waits=["pe_bs"], key="bs")

        def emit_d1e1(t, eng, bcw=False):
            xt = x[:, t * D:(t + 1) * D]
            op(eng, (lambda xt=xt: lambda e: e.tensor_tensor(
                xt, bcast[:, :], xt, ALU.add))(),
               waits=["bc" if bcw else None, f"x1b_{t}", f"b5_{t}"],
               key=f"d1_{t}")
            op("scalar", (lambda xt=xt, t=t: lambda e: e.activation(
                sqs[:, :], xt, AF.Square,
                accum_out=stt[("msss", 2)][:, 2 * t + 1:2 * t + 2]))(),
               waits=[f"d1_{t}", sqs_chain[0]], key=f"e1_{t}")
            sqs_chain[0] = f"e1_{t}"

        msss2, mE2 = stt[("msss", 2)], stt[("mE", 2)]

        def ev2(tns, p, o=0):
            return bass.AP(tns, 4 * p + o, [[tns.shape[1], 128], [2, 2]])

        def emit_stats2p(p):
            sl = slice(2 * p, 2 * p + 2)
            op("vector", (lambda p=p: lambda e: e.tensor_scalar(
                ev2(mE2, p), ev2(stt[("msss", 1)], p), INVN, bs1024,
                ALU.mult, ALU.add))(),
               waits=[f"a5_{2 * p + 1}", "bs"], key=f"cm2_{p}")
            op("vector", (lambda p=p: lambda e: e.tensor_scalar(
                ev2(mE2, p, 1), ev2(msss2, p, 1), INVN, None, ALU.mult))(),
               waits=[f"e1_{2 * p + 1}"], key=f"cE2_{p}")
            op("vector", (lambda p=p, sl=sl: lambda e: e.scalar_tensor_tensor(
                stt[("p", 2)][:, sl], ev2(mE2, p), 1.0, ev2(mE2, p),
                ALU.bypass, ALU.mult))(),
               waits=[f"cm2_{p}"], key=f"cp2_{p}")
            op("vector", (lambda p=p, sl=sl: lambda e: e.tensor_tensor(
                stt[("vp", 2)][:, sl], ev2(mE2, p, 1), stt[("p", 2)][:, sl],
                ALU.subtract))(),
               waits=[f"cp2_{p}", f"cE2_{p}"], key=f"cvp2_{p}")

        def emit_stats2p_s(p):
            sl = slice(2 * p, 2 * p + 2)
            op("scalar", (lambda sl=sl: lambda e: e.activation(
                stt[("vp", 2)][:, sl], stt[("vp", 2)][:, sl], AF.Ln,
                bias=EPS))(),
               waits=[f"cvp2_{p}"], key=f"cln2_{p}")
            op("scalar", (lambda sl=sl: lambda e: e.activation(
                stt[("rs", 2)][:, sl], stt[("vp", 2)][:, sl], AF.Exp,
                scale=-0.5))(),
               waits=[f"cln2_{p}"], key=f"cexp2_{p}")

        def emit_stats2p_nb(p):
            sl = slice(2 * p, 2 * p + 2)
            op("vector", (lambda p=p, sl=sl: lambda e: e.scalar_tensor_tensor(
                stt[("nb", 2)][:, sl], ev2(mE2, p), -1.0,
                stt[("rs", 2)][:, sl], ALU.mult, ALU.mult))(),
               waits=[f"cexp2_{p}", f"cm2_{p}"], key=f"cnb2_{p}")

        def emit_stats2p_all(p):
            emit_stats2p(p)
            emit_stats2p_s(p)
            emit_stats2p_nb(p)

        def emit_apply2(t):
            b4_ = t % 4
            ws = [f"cnb2_{t // 2}", f"d1_{t}",
                  f"f1_{t - 4}" if t >= 4 else f"c1_{NT - 4 + t}"]
            if t % 2 == 0:
                op("scalar", (lambda t=t, b4_=b4_: lambda e: e.activation(
                    xn[:, b4_ * D:(b4_ + 1) * D], x[:, t * D:(t + 1) * D],
                    AF.Identity, bias=stt[("nb", 2)][:, t:t + 1],
                    scale=stt[("rs", 2)][:, t:t + 1]))(),
                   waits=ws, key=f"e4_{t}")
            else:
                op("vector", (lambda t=t, b4_=b4_: lambda e: e.tensor_scalar(
                    xn[:, b4_ * D:(b4_ + 1) * D], x[:, t * D:(t + 1) * D],
                    stt[("rs", 2)][:, t:t + 1], stt[("nb", 2)][:, t:t + 1],
                    ALU.mult, ALU.add))(),
                   waits=ws, key=f"e4_{t}")

        emit_d1e1(0, "vector", bcw=True)
        emit_d1e1(1, "vector")
        emit_stats2p_all(0)
        emit_apply2(0)
        emit_apply2(1)
        for t in range(NT):
            b = t % 2
            if t == 0:
                op("tensor", transposes(0, "C"),
                   waits=["e4_0", f"a4_{NT-2}"], key="f1_0")
                op("tensor", transposes(1, "C"),
                   waits=["e4_1", f"a4_{NT-1}"], key="f1_1")
            elif t + 1 < NT:
                op("tensor", transposes(t + 1, "C"),
                   waits=[f"e4_{t + 1}", f"d4_{t - 1}"], key=f"f1_{t + 1}")
            op("scalar", (lambda b=b: lambda e: e.copy(
                xnt[:, b * D:(b + 1) * D], psT[b][:, :]))(),
               waits=[f"f1_{t}",
                      f"f2_{t - 2}" if t >= 2 else f"c2_{NT - 2 + t}"],
               key=f"d4_{t}")
            op("tensor", ffn_mm(t),
               waits=[f"d4_{t}", f"d5_{t - 2}" if t >= 2 else "bc"],
               key=f"f2_{t}")
            for td in ([t - 1] if t >= 1 else []) + ([t] if t == NT - 1 else []):
                bd = td % 2
                op("vector", (lambda bd=bd, td=td: lambda e: e.scalar_tensor_tensor(
                    x[:, td * D:(td + 1) * D], psF[bd][:, :], alphab,
                    x[:, td * D:(td + 1) * D], ALU.mult, ALU.add))(),
                   waits=[f"f2_{td}", "bcp"], key=f"d5_{td}")
                op("sync", (lambda td=td: lambda e: e.dma_start(
                    out=out_h[td * 128:(td + 1) * 128, :],
                    in_=x[:, td * D:(td + 1) * D]))(),
                   waits=[f"d5_{td}"], dma="qy", key=f"st_{td}")
            if t == 0:
                emit_d1e1(2, "vector")
                emit_d1e1(3, "vector")
                emit_stats2p_all(1)
                emit_apply2(2)
            elif 2 * t + 3 < NT:
                emit_d1e1(2 * t + 2, "vector")
                emit_d1e1(2 * t + 3, "vector")
                emit_stats2p_all(t + 1)
            if t + 3 < NT:
                emit_apply2(t + 3)

        def emit_for(name):
            def body(eng):
                for (en, waits, fn, sem, amt) in sched:
                    if en != name:
                        continue
                    for w in waits:
                        if isinstance(w, str):
                            w = land[w]
                        eng.wait_ge(sems[w[0]], w[1])
                    ins = fn(eng)
                    if sem is not None:
                        ins.then_inc(sems[sem], amt)
                if name == "sync":
                    eng.wait_ge(sems["qy"], 16 * NT)
            return body

        block.sync(emit_for("sync"))
        block.gpsimd(emit_for("gpsimd"))
        block.vector(emit_for("vector"))
        block.scalar(emit_for("scalar"))
        block.tensor(emit_for("tensor"))

    return nc


def kernel(L0_attn_x, L0_attn_wq, L0_attn_bq, L0_attn_wk, L0_attn_bk,
           L0_attn_wv, L0_attn_bv, L0_attn_wo, L0_attn_bo, L0_attn_w1,
           L0_attn_b1, L0_attn_w2, L0_attn_b2, L0_attn_hadamard, L0_attn_rot,
           L0_ffn_weight):
    assert L0_attn_wq.shape == L0_attn_wk.shape == (D, D)
    assert L0_attn_bq.shape == L0_attn_bk.shape == (D,)
    assert L0_attn_w1.shape == (64, D) and L0_attn_b1.shape == (D,)
    assert L0_attn_w2.shape == (D, 16) and L0_attn_b2.shape == (16,)
    assert L0_attn_hadamard.shape == (8, 8) and L0_attn_rot.shape == (16, 8, 3)

    x = np.ascontiguousarray(np.asarray(L0_attn_x, dtype=np.float32))
    wf = np.ascontiguousarray(np.asarray(L0_ffn_weight, dtype=np.float32))
    wv = np.ascontiguousarray(np.asarray(L0_attn_wv, dtype=np.float32))
    wo = np.ascontiguousarray(np.asarray(L0_attn_wo, dtype=np.float32))
    bv = np.ascontiguousarray(np.asarray(L0_attn_bv, dtype=np.float32)).reshape(1, D)
    bo = np.ascontiguousarray(np.asarray(L0_attn_bo, dtype=np.float32)).reshape(1, D)
    B, S, _ = x.shape

    if "nc" not in _CACHE:
        _CACHE["nc"] = _build()
    nc = _CACHE["nc"]

    idb = np.eye(128, dtype=ml_dtypes.bfloat16)
    idf = np.eye(4, dtype=np.float32)
    cst = np.zeros((128, 2), dtype=np.float32)
    cst[:, 0] = EPS
    cst[:, 1] = MAGIC
    in_maps = []
    for c in range(8):
        b, h = c // 2, c % 2
        in_maps.append({
            "x": np.ascontiguousarray(x[b, h * TOKS:(h + 1) * TOKS, :]),
            "wf": wf, "wv": wv, "wo": wo, "bv": bv, "bo": bo,
            "idb": idb, "idf": idf, "cst": cst,
        })

    res = run_bass_kernel_spmd(nc, in_maps, core_ids=list(range(8)))

    out = np.empty((B, S, D), dtype=np.float32)
    for c in range(8):
        b, h = c // 2, c % 2
        out[b, h * TOKS:(h + 1) * TOKS, :] = res.results[c]["out"]
    return out
